# version 01
# baseline (speedup 1.0000x reference)
import contextlib
import numpy as np
import concourse.bass as bass
import concourse.mybir as mybir
from concourse.bass_utils import run_bass_kernel_spmd

F32 = mybir.dt.float32
BF16 = mybir.dt.bfloat16
AF = mybir.ActivationFunctionType
ALU = mybir.AluOpType
AX = mybir.AxisListType

B, L, D, HALF, NFFT, NE = 4, 4096, 1024, 2048, 4096, 8
DG = 128
NGRP = D // DG
TILE = 512
NCH = 8
P2_GROUPS = NGRP
P2_STAGE = 9

FS_S2 = 0
FS_I2 = 12288
FS_I1 = 16384
FS_N = 16576


_SCALE_CACHE = None


def _scale_bits():
    global _SCALE_CACHE
    if _SCALE_CACHE is not None:
        return _SCALE_CACHE
    import jax
    import jax.numpy as jnp
    cpu = jax.devices("cpu")[0]
    with jax.default_device(cpu):
        s = jnp.power(jnp.float32(0.001), jnp.linspace(0.0, 1.0, D, dtype=jnp.float32))
        _SCALE_CACHE = np.asarray(s, dtype=np.float32)
    return _SCALE_CACHE


def _funky_perm():
    perm = np.zeros(HALF, dtype=np.int64)
    for a in range(16):
        for j in range(4):
            for nB in range(32):
                nA = 4 * a + j
                perm[128 * a + 32 * j + nB] = nA + 64 * nB
    return perm


FUNKY = _funky_perm()
INV_FUNKY = np.argsort(FUNKY)


def _c64_tables(scale):
    ln32 = (np.complex64(np.float32(1e-5)) + 1j * scale.astype(np.complex64)).astype(np.complex64)
    t32 = np.arange(L, dtype=np.float32)
    P32 = np.exp(t32[:, None].astype(np.complex64) * ln32[None, :]).astype(np.complex64)
    MU32 = (P32 * np.exp(ln32)[None, :]).astype(np.complex64)
    K = P32.real.astype(np.float64)
    return K, MU32


def _kf_spec(Kseg):
    full = np.zeros((NFFT, D))
    full[:HALF] = Kseg
    F = np.fft.fft(full, axis=0)
    kA = np.arange(NFFT) // 64
    kB = np.arange(NFFT) % 64
    out = np.zeros((64, 64, D), dtype=np.complex128)
    out[kA, kB] = F
    return out


def _table_layout(T):
    out = np.zeros((NGRP, 128, 64 * 2 * 64), dtype=np.float32)
    kA = np.arange(64)
    for j in range(2):
        for dq in range(64):
            for grp in range(NGRP):
                d = grp * 128 + 2 * dq + j
                col = np.arange(64) * 128 + dq
                rows = kA + 64 * j
                out[grp, rows[:, None], col[None, :]] = \
                    T[:, :, d].real.astype(np.float32)
                out[grp, rows[:, None], col[None, :] + 64] = \
                    T[:, :, d].imag.astype(np.float32)
    return out


_W = lambda N: np.exp(-2j * np.pi / N)


def _fft_const_tables():
    a = np.arange(64)
    W64 = _W(64) ** (a[:, None] * a[None, :])
    w1 = np.zeros((32, 128), dtype=np.float32)
    w1[:, :64] = W64[:32, :].real
    w1[:, 64:] = W64[:32, :].imag
    tw = _W(NFFT) ** (a[:, None] * a[None, :])
    stats = np.zeros((64, FS_N), dtype=np.float32)
    for kb in range(64):
        M = W64 * tw[:, kb][:, None]
        stats[:, FS_S2 + 0 * 4096 + kb * 64: FS_S2 + 0 * 4096 + kb * 64 + 64] = \
            M.real.astype(np.float32)
        stats[:, FS_S2 + 1 * 4096 + kb * 64: FS_S2 + 1 * 4096 + kb * 64 + 64] = \
            M.imag.astype(np.float32)
        stats[:, FS_S2 + 2 * 4096 + kb * 64: FS_S2 + 2 * 4096 + kb * 64 + 64] = \
            (-M.imag).astype(np.float32)
    W64i = np.conj(W64)
    twi = np.conj(tw)
    for nA in range(64):
        S = (W64i[:, :32] * twi[nA, :][:, None]) / NFFT
        stats[:, FS_I2 + nA * 32: FS_I2 + nA * 32 + 32] = S.real.astype(np.float32)
        stats[:, FS_I2 + 2048 + nA * 32: FS_I2 + 2048 + nA * 32 + 32] = \
            (-S.imag).astype(np.float32)
    stats[:, FS_I1: FS_I1 + 64] = W64i.real.astype(np.float32)
    stats[:, FS_I1 + 64: FS_I1 + 128] = W64i.imag.astype(np.float32)
    stats[:, FS_I1 + 128: FS_I1 + 192] = (-W64i.imag).astype(np.float32)
    full = np.concatenate([stats, stats], axis=0)
    return w1, np.ascontiguousarray(full)



ENGINES = ("sync", "tensor", "vector", "scalar", "gpsimd")


class Sem:
    def __init__(self, name):
        self.name = name
        self.count = 0
        self.handle = None


class Op:
    __slots__ = ("engine", "fn", "waits", "incs")

    def __init__(self, engine, fn, waits, incs):
        self.engine = engine
        self.fn = fn
        self.waits = waits
        self.incs = incs


class Sched:
    def __init__(self):
        self.ops = []
        self.sems = []
        self.pending = {e: [] for e in ENGINES}

    def sem(self, name):
        s = Sem(name)
        self.sems.append(s)
        return s

    def op(self, engine, fn, waits=(), inc=None, inc_amt=1):
        w = list(self.pending[engine]) + [x for x in waits if x is not None]
        self.pending[engine] = []
        incs = []
        if inc is not None:
            inc.count += inc_amt
            incs.append((inc, inc_amt))
        o = Op(engine, fn, w, incs)
        self.ops.append(o)
        return o

    def dma(self, engine, out_ap, in_ap, waits=(), inc=None):
        def fn(eng, out_ap=out_ap, in_ap=in_ap):
            return eng.dma_start(out=out_ap, in_=in_ap)
        return self.op(engine, fn, waits=waits, inc=inc, inc_amt=16)

    def after(self, sem):
        return (sem, sem.count)

    def fence_all(self):
        marks = [self.after(s) for s in self.sems if s.count > 0]
        for e in ENGINES:
            self.pending[e].extend(marks)

    def emit(self, block):
        def run(engine_name, eng):
            for o in self.ops:
                if o.engine != engine_name:
                    continue
                for (s, v) in o.waits:
                    if v > 0:
                        eng.wait_ge(s.handle, v)
                if o.fn is None:
                    continue
                inst = o.fn(eng)
                for (s, amt) in o.incs:
                    inst.then_inc(s.handle, amt)

        @block.sync
        def _(eng):
            run("sync", eng)

        @block.tensor
        def _(eng):
            run("tensor", eng)

        @block.vector
        def _(eng):
            run("vector", eng)

        @block.scalar
        def _(eng):
            run("scalar", eng)

        @block.gpsimd
        def _(eng):
            run("gpsimd", eng)


class Arena:
    def __init__(self, tensor, nelem):
        self.t = tensor
        self.n = nelem
        self.marks = []
        self.off = 0

    def alloc(self, nelem, tag=""):
        off = self.off
        assert off + nelem <= self.n, f"SBUF arena overflow at {tag}: {off}+{nelem}>{self.n}"
        self.off += nelem
        return off

    def push(self):
        self.marks.append(self.off)

    def pop(self):
        self.off = self.marks.pop()

    def f32(self, off, nelem, parts=128, p0=0):
        return self.t[p0:p0 + parts, off:off + nelem]

    def bf16(self, off, nelem_bf16, parts=128, p0=0):
        assert nelem_bf16 % 2 == 0
        return self.t[p0:p0 + parts, off:off + nelem_bf16 // 2].bitcast(BF16)



def build_nc(debug_phase=99):
    nc = bass.Bass("TRN2", target_bir_lowering=False, debug=False)

    def dram(name, shape, kind="ExternalInput", dtype=F32):
        return nc.dram_tensor(name, list(shape), dtype, kind=kind).ap()

    T = {}
    T["xf"] = dram("x_full", (L, D))
    T["xo"] = dram("x_own", (HALF, D))
    for n, shape in [("wzt", (D, D)), ("wzat", (D, D)), ("wyt", (D, D)),
                     ("wyat", (D, D)), ("wst", (D, D)), ("wet", (NE * D, D)),
                     ("wrt", (D, NE)), ("bz_c", (128, NCH)), ("bza_c", (128, NCH)),
                     ("by_c", (128, NCH)), ("bya_c", (128, NCH)), ("b9", (16, D)),
                     ("brb", (128, NE)), ("nwb", (128, D)), ("lcreb", (128, D)),
                     ("lcimb", (128, D)), ("mur", (HALF, D)), ("mui", (HALF, D)),
                     ("w1t", (32, 128)), ("fstats", (128, FS_N)),
                     ("tat", (NGRP, 128, 8192)), ("tbt", (NGRP, 128, 8192)),
                     ("identt", (128, 192)), ("epsc", (128, 2))]:
        T[n] = dram(n, shape)

    sk = "ExternalOutput" if debug_phase < 99 else "Internal"
    T["z_s"] = dram("z_s", (D, L), kind=sk)
    T["g_s"] = dram("g_s", (D, HALF), kind=sk)
    T["h_s"] = dram("h_s", (HALF, D), kind=sk)
    T["x1_s"] = dram("x1_s", (HALF, D), kind=sk)
    T["x1f_s"] = dram("x1f_s", (2, 4, 128, 4096), kind="Internal", dtype=BF16)
    T["wsp_s"] = dram("wsp_s", (2, 9 * D, D), kind="Internal", dtype=BF16)
    T["out_t"] = dram("out", (HALF, D), kind="ExternalOutput")

    S = Sched()
    ARENA_N = 33024

    with contextlib.ExitStack() as stack:
        arena_t = stack.enter_context(nc.sbuf_tensor("arena", [128, ARENA_N], F32))
        small_t = stack.enter_context(nc.sbuf_tensor("small", [128, 3600], F32))
        fst_t = stack.enter_context(nc.sbuf_tensor("fst", [128, FS_N], F32))
        psum = [stack.enter_context(nc.psum_tensor(f"ps{i}", [128, 512], F32))
                for i in range(8)]
        sems = {}
        for n in ["ld", "ld2", "st", "mm", "ev", "sc", "vc", "gp"]:
            s = S.sem(n)
            s.handle = stack.enter_context(nc.semaphore(n))
            sems[n] = s

        A = Arena(arena_t, ARENA_N)
        emit_program(S, A, sems, psum, small_t, fst_t, T, debug_phase)

        with nc.Block() as block:
            S.emit(block)
    return nc


def emit_program(S, A, sems, psum, small, fst, T, debug_phase):
    ld, ld2, st = sems["ld"], sems["ld2"], sems["st"]
    mm, ev, sc, vc, gp = sems["mm"], sems["ev"], sems["sc"], sems["vc"], sems["gp"]
    xf, xo = T["xf"], T["xo"]
    z_s, g_s, h_s, x1_s = T["z_s"], T["g_s"], T["h_s"], T["x1_s"]
    x1f_s, wsp, out_t = T["x1f_s"], T["wsp_s"], T["out_t"]

    def finish(extra_waits=()):
        S.op("gpsimd", None, waits=[S.after(st)] + list(extra_waits))

    SM_BZ, SM_BZA, SM_BY, SM_BYA, SM_BRB = 0, 8, 16, 24, 32
    SM_NWB, SM_LCR, SM_LCI, SM_ID = 64, 1088, 2112, 3136
    SM_G, SM_SC, SM_EPS = 3340, 3450, 3550
    for off, key in [(SM_BZ, "bz_c"), (SM_BZA, "bza_c"), (SM_BY, "by_c"),
                     (SM_BYA, "bya_c"), (SM_BRB, "brb")]:
        S.dma("sync", small[:, off:off + 8], T[key][:, :], inc=ld)
    for off, key in [(SM_NWB, "nwb"), (SM_LCR, "lcreb"), (SM_LCI, "lcimb")]:
        S.dma("sync", small[:, off:off + D], T[key][:, :], inc=ld)
    S.dma("sync", small[:, SM_ID:SM_ID + 192], T["identt"][:, :], inc=ld)
    S.dma("sync", small[:, SM_EPS:SM_EPS + 2], T["epsc"][:, :], inc=ld)
    smalls_done = S.after(ld)
    ident = small[:, SM_ID:SM_ID + 128]
    ident64 = [small[0:64, SM_ID + 128:SM_ID + 192],
               small[64:128, SM_ID + 128:SM_ID + 192]]

    MMKW = dict(skip_group_check=True)

    def load_weight_split(wdram, hi_off, lo_off, stage_off, stage_n=4):
        hi = A.bf16(hi_off, 8 * D)
        lo = A.bf16(lo_off, 8 * D)
        for c in range(NCH):
            sl = c % stage_n
            stg = A.f32(stage_off + sl * D, D)
            S.dma("sync", stg, wdram[c * 128:(c + 1) * 128, :],
                  waits=[S.after(gp)], inc=ld)
            w = S.after(ld)
            S.op("gpsimd", lambda e, o=hi[:, c * D:(c + 1) * D], i=stg:
                 e.tensor_copy(o, i), waits=[w], inc=gp)
            S.op("gpsimd", lambda e, o=lo[:, c * D:(c + 1) * D], i0=stg,
                 i1=hi[:, c * D:(c + 1) * D]:
                 e.tensor_tensor(o, i0, i1, ALU.subtract), inc=gp)
        return S.after(gp)

    def mm3(bank, stat_hi, stat_lo, mov_hi, mov_lo, start, stop, waits=(), inc=None):
        outap = psum[bank][:, :]
        S.op("tensor", lambda e, o=outap, s=stat_hi, m=mov_hi, a=start:
             e.matmul(o, s, m, start=a, stop=False, **MMKW), waits=list(waits))
        S.op("tensor", lambda e, o=outap, s=stat_hi, m=mov_lo:
             e.matmul(o, s, m, start=False, stop=False, **MMKW))
        return S.op("tensor", lambda e, o=outap, s=stat_lo, m=mov_hi, b=stop:
                    e.matmul(o, s, m, start=False, stop=b, **MMKW), inc=inc)

    def wtile(hi_off, lo_off, dc, hc):
        hi = A.bf16(hi_off, 8 * D)
        lo = A.bf16(lo_off, 8 * D)
        return (hi[:, dc * D + hc * 128: dc * D + hc * 128 + 128],
                lo[:, dc * D + hc * 128: dc * D + hc * 128 + 128])

    def transpose_split_tile(src_f32_off, dsth_off, dstl_off, first_waits=(),
                             nelem=8 * 512):
        for dc in range(NCH):
            bank = dc % 2
            for c4 in range(4):
                wt = []
                if dc == 0 and c4 == 0:
                    wt = list(first_waits) + [smalls_done]
                if c4 == 0:
                    wt.append(S.after(ev))
                S.op("tensor", lambda e, b=bank, c4=c4, dc=dc:
                     e.transpose(psum[b][:, c4 * 128:(c4 + 1) * 128],
                                 A.f32(src_f32_off + c4 * D + dc * 128, 128), ident),
                     waits=wt, inc=(mm if c4 == 3 else None))
            wmm = S.after(mm)
            hi = A.bf16(dsth_off, nelem)[:, dc * 512:(dc + 1) * 512]
            lo = A.bf16(dstl_off, nelem)[:, dc * 512:(dc + 1) * 512]
            S.op("scalar", lambda e, o=hi, b=bank: e.copy(o, psum[b][:, :]),
                 waits=[wmm], inc=sc)
            S.op("vector", lambda e, o=lo, b=bank, h=hi:
                 e.tensor_tensor(o, psum[b][:, :], h, ALU.subtract),
                 waits=[wmm, S.after(sc)], inc=ev)
        return S.after(ev)

    def proj_tile(whi, wlo, xth, xtl, bank, hc, extra=()):
        xh = A.bf16(xth, 8 * 512)
        xl = A.bf16(xtl, 8 * 512)
        for dc in range(NCH):
            sh, sl = wtile(whi, wlo, dc, hc)
            w = list(extra) if dc == 0 else []
            mm3(bank, sh, sl, xh[:, dc * 512:(dc + 1) * 512],
                xl[:, dc * 512:(dc + 1) * 512],
                start=(dc == 0), stop=(dc == NCH - 1), waits=w,
                inc=(mm if dc == NCH - 1 else None))
        return S.after(mm)

    A.push()
    stage_off = A.alloc(4 * D, "wstage")
    wa_hi = A.alloc(4 * D, "wa_hi"); wa_lo = A.alloc(4 * D, "wa_lo")
    wb_hi = A.alloc(4 * D, "wb_hi"); wb_lo = A.alloc(4 * D, "wb_lo")
    xt_off = A.alloc(4 * D, "xt")
    xfh_off = A.alloc(2 * D, "xf_hi")
    xfl_off = A.alloc(2 * D, "xf_lo")
    zsc_off = A.alloc(6 * 512, "z_scr")

    wz_done = load_weight_split(T["wzt"], wa_hi, wa_lo, stage_off)
    wza_done = load_weight_split(T["wzat"], wb_hi, wb_lo, stage_off)

    def load_ttile(src, t0):
        for c4 in range(4):
            S.dma("sync", A.f32(xt_off + c4 * D, D),
                  src[t0 + c4 * 128: t0 + c4 * 128 + 128, :],
                  waits=[S.after(mm)] if c4 == 0 else (), inc=ld)
        return S.after(ld)

    for tau in range(8):
        xw = load_ttile(xf, tau * TILE)
        xdone = transpose_split_tile(xt_off, xfh_off, xfl_off, first_waits=[xw])
        for hc in range(NCH):
            b1, b2 = 2 + (hc % 3) * 2, 3 + (hc % 3) * 2
            extra = [xdone]
            if hc == 0:
                extra += [wz_done, wza_done]
            if hc >= 3:
                extra.append(S.after(vc))
            proj_tile(wa_hi, wa_lo, xfh_off, xfl_off, b1, hc, extra)
            m2 = proj_tile(wb_hi, wb_lo, xfh_off, xfl_off, b2, hc)
            par = hc % 2
            z1b = A.f32(zsc_off + par * 1536, 512)
            z2sB = A.f32(zsc_off + par * 1536 + 512, 512)
            zfB = A.f32(zsc_off + par * 1536 + 1024, 512)
            stw = [S.after(st)] if (tau > 0 or hc >= 2) else []
            S.op("scalar", lambda e, b=b2, hc=hc, o=z2sB:
                 e.activation(out=o, in_=psum[b][:, :], func=AF.Silu,
                              bias=small[:, SM_BZA + hc:SM_BZA + hc + 1], scale=1.0),
                 waits=[m2] + stw, inc=sc)
            S.op("scalar", lambda e, b=b1, hc=hc, o=z1b:
                 e.activation(out=o, in_=psum[b][:, :], func=AF.Identity,
                              bias=small[:, SM_BZ + hc:SM_BZ + hc + 1], scale=1.0),
                 inc=sc)
            S.op("vector", lambda e, o=zfB, i0=z1b, i1=z2sB:
                 e.tensor_tensor(o, i0, i1, ALU.mult),
                 waits=[S.after(sc)], inc=vc)
            S.dma("gpsimd", z_s[hc * 128:(hc + 1) * 128, tau * TILE:(tau + 1) * TILE],
                  zfB, waits=[S.after(vc)], inc=st)

    wya_done = load_weight_split(T["wyat"], wa_hi, wa_lo, stage_off)
    for tau in range(4):
        xw = load_ttile(xo, tau * TILE)
        xdone = transpose_split_tile(xt_off, xfh_off, xfl_off, first_waits=[xw])
        for hc in range(NCH):
            b1 = 2 + (hc % 3) * 2
            extra = [xdone] + ([wya_done] if hc == 0 else [])
            if hc >= 3:
                extra.append(S.after(sc))
            m1 = proj_tile(wa_hi, wa_lo, xfh_off, xfl_off, b1, hc, extra)
            par = hc % 2
            gB = A.f32(zsc_off + par * 1536, 512)
            stw = [S.after(st)] if (tau > 0 or hc >= 2) else []
            S.op("scalar", lambda e, b=b1, hc=hc, o=gB:
                 e.activation(out=o, in_=psum[b][:, :], func=AF.Silu,
                              bias=small[:, SM_BYA + hc:SM_BYA + hc + 1], scale=1.0),
                 waits=[m1] + stw, inc=sc)
            S.dma("gpsimd", g_s[hc * 128:(hc + 1) * 128, tau * TILE:(tau + 1) * TILE],
                  gB, waits=[S.after(sc)], inc=st)

    A.pop()
    S.fence_all()
    if debug_phase <= 1:
        finish()
        return

    A.push()
    zin_off = A.alloc(8192, "zin")
    b1_off = A.alloc(8192, "B1_E_scr")
    ca_off = A.alloc(8192, "C_A_D_E2")
    cb_off = A.alloc(8192, "C_B")
    w1_off = A.alloc(128, "w1")
    tq_off = zin_off
    hsl_off = zin_off + 4096

    S.dma("sync", A.f32(w1_off, 128, parts=32), T["w1t"][:, :], inc=ld)
    S.dma("sync", fst[:, :], T["fstats"][:, :], inc=ld)
    fft_tables = S.after(ld)

    w1ap = A.f32(w1_off, 128, parts=32)

    def s2ap(kind, kb, p0):
        return fst[p0:p0 + 64, FS_S2 + kind * 4096 + kb * 64:
                   FS_S2 + kind * 4096 + kb * 64 + 64]

    def i2ap(kind, nA, p0):
        return fst[p0:p0 + 64, FS_I2 + kind * 2048 + nA * 32:
                   FS_I2 + kind * 2048 + nA * 32 + 32]

    def i1ap(kind, p0):
        return fst[p0:p0 + 64, FS_I1 + kind * 64: FS_I1 + kind * 64 + 64]

    for grp in range(P2_GROUPS):
        for seg in range(2):
            c_off = ca_off if seg == 0 else cb_off
            zsrc = z_s[grp * 128:(grp + 1) * 128, seg * HALF:(seg + 1) * HALF]
            zw_wait = [S.after(st), S.after(vc), S.after(gp), S.after(mm)]
            if grp == 0 and seg == 0:
                zw_wait.append(fft_tables)
            for sub in range(4):
                zv = zsrc[sub * 32:(sub + 1) * 32, :] \
                    .rearrange("d (nb na) -> nb d na", na=64)
                S.dma("sync", A.f32(zin_off + sub * 32 * 64, 32 * 64, parts=32), zv,
                      waits=zw_wait if sub == 0 else (), inc=ld)
            zw = S.after(ld)
            for dq in range(64):
                bank = (dq // 4) % 2
                for j in range(2):
                    d_idx = 2 * dq + j
                    stat = A.f32(zin_off + d_idx * 64, 64, parts=32)
                    wt = []
                    if j == 0 and dq % 4 == 0:
                        wt.append(S.after(ev))
                        if dq == 0:
                            wt.append(zw)
                    S.op("tensor", lambda e, b=bank, dq=dq, j=j, s=stat:
                         e.matmul(psum[b][64 * j:64 * j + 64,
                                          (dq % 4) * 128:(dq % 4) * 128 + 128],
                                  s, w1ap, start=True, stop=True,
                                  tile_position=(0, 64 * j), **MMKW),
                         waits=wt, inc=(mm if (dq % 4 == 3 and j == 1) else None))
                if dq % 4 == 3:
                    wmm = S.after(mm)
                    dq0 = dq - 3
                    dst = A.f32(b1_off, 8192).rearrange("p (kb q) -> p kb q", q=64)
                    S.op("vector", lambda e, b=bank, dq0=dq0, dst=dst:
                         e.tensor_copy(
                             dst[:, :, dq0:dq0 + 4].rearrange("p kb q -> p q kb"),
                             psum[b][:, :].rearrange("p (q kb) -> p q kb", q=4)),
                         waits=[wmm], inc=ev)
            b1done = S.after(ev)
            if P2_STAGE < 2:
                continue
            for kb in range(64):
                bre = 4 + (kb % 2) * 2
                bim = 5 + (kb % 2) * 2
                b1v = A.f32(b1_off, 8192)
                mre = [b1v[0:64, kb * 64:kb * 64 + 64],
                       b1v[64:128, kb * 64:kb * 64 + 64]]
                mim = [b1v[0:64, (64 + kb) * 64:(64 + kb) * 64 + 64],
                       b1v[64:128, (64 + kb) * 64:(64 + kb) * 64 + 64]]
                wt0 = ([b1done] if kb == 0 else []) + \
                    ([S.after(ev)] if kb >= 2 else [])
                for bi, (bnk, kA_, kB_) in enumerate([(bre, 0, 2), (bim, 1, 0)]):
                    for j in range(2):
                        S.op("tensor", lambda e, b=bnk, j=j, k=kA_, kb=kb,
                             m=mre[j]:
                             e.matmul(psum[b][64 * j:64 * j + 64, 0:64],
                                      s2ap(k, kb, 64 * j), m,
                                      start=True, stop=False,
                                      tile_position=(64 * j, 64 * j), **MMKW),
                             waits=wt0 if (bi == 0 and j == 0) else [])
                        S.op("tensor", lambda e, b=bnk, j=j, k=kB_, kb=kb,
                             m=mim[j]:
                             e.matmul(psum[b][64 * j:64 * j + 64, 0:64],
                                      s2ap(k, kb, 64 * j), m,
                                      start=False, stop=True,
                                      tile_position=(64 * j, 64 * j), **MMKW),
                             inc=(mm if (bi == 1 and j == 1) else None))
                wmm = S.after(mm)
                cv = A.f32(c_off, 8192)
                S.op("vector", lambda e, b=bre, kb=kb, cv=cv:
                     e.tensor_copy(cv[:, kb * 128:kb * 128 + 64], psum[b][:, 0:64]),
                     waits=[wmm], inc=ev)
                S.op("vector", lambda e, b=bim, kb=kb, cv=cv:
                     e.tensor_copy(cv[:, kb * 128 + 64:kb * 128 + 128],
                                   psum[b][:, 0:64]), inc=ev)
        cdone = S.after(ev)
        if P2_STAGE < 3:
            continue
        for s4 in range(4):
            tw0 = [cdone, S.after(mm)] if s4 == 0 else [S.after(vc), S.after(gp)]
            S.dma("scalar", A.f32(tq_off, 2048),
                  T["tat"][grp, :, s4 * 2048:(s4 + 1) * 2048], waits=tw0, inc=ld2)
            S.dma("scalar", A.f32(tq_off + 2048, 2048),
                  T["tbt"][grp, :, s4 * 2048:(s4 + 1) * 2048], inc=ld2)
            tabs = S.after(ld2)
            cav = A.f32(ca_off + s4 * 2048, 2048) \
                .rearrange("p (kb pl q) -> p kb pl q", pl=2, q=64)
            cbv = A.f32(cb_off + s4 * 2048, 2048) \
                .rearrange("p (kb pl q) -> p kb pl q", pl=2, q=64)
            tav = A.f32(tq_off, 2048).rearrange("p (kb pl q) -> p kb pl q", pl=2, q=64)
            tbv = A.f32(tq_off + 2048, 2048) \
                .rearrange("p (kb pl q) -> p kb pl q", pl=2, q=64)
            are, aim = cav[:, :, 0, :], cav[:, :, 1, :]
            brev, bimv = cbv[:, :, 0, :], cbv[:, :, 1, :]
            tare, taim = tav[:, :, 0, :], tav[:, :, 1, :]
            tbre, tbim = tbv[:, :, 0, :], tbv[:, :, 1, :]
            sc1 = A.f32(b1_off, 1024).rearrange("p (kb q) -> p kb q", q=64)
            sc2v = A.f32(b1_off + 1024, 1024).rearrange("p (kb q) -> p kb q", q=64)
            p2 = A.f32(b1_off + 2048, 1024).rearrange("p (kb q) -> p kb q", q=64)
            p3 = A.f32(b1_off + 3072, 1024).rearrange("p (kb q) -> p kb q", q=64)

            def tt(eng, out, x, y, op, waits=(), inc=None):
                S.op(eng, lambda e, o=out, a=x, b=y, op=op:
                     e.tensor_tensor(o, a, b, op), waits=list(waits), inc=inc)

            tt("vector", sc1, are, tare, ALU.mult, waits=[tabs])
            tt("gpsimd", sc2v, are, taim, ALU.mult, waits=[tabs])
            tt("vector", p2, aim, taim, ALU.mult)
            tt("vector", sc1, sc1, p2, ALU.subtract)
            tt("gpsimd", p3, aim, tare, ALU.mult)
            tt("gpsimd", sc2v, sc2v, p3, ALU.add)
            tt("vector", p2, brev, tbre, ALU.mult)
            tt("vector", sc1, sc1, p2, ALU.add)
            tt("gpsimd", p3, brev, tbim, ALU.mult)
            tt("gpsimd", sc2v, sc2v, p3, ALU.add)
            tt("vector", p2, bimv, tbim, ALU.mult)
            tt("vector", sc1, sc1, p2, ALU.subtract, inc=vc)
            tt("gpsimd", p3, bimv, tbre, ALU.mult)
            tt("gpsimd", sc2v, sc2v, p3, ALU.add, inc=gp)
            S.op("vector", lambda e, o=are, i=sc1: e.tensor_copy(o, i),
                 waits=[S.after(gp)], inc=vc)
            S.op("gpsimd", lambda e, o=aim, i=sc2v: e.tensor_copy(o, i),
                 waits=[S.after(vc)], inc=gp)
        ddone = [S.after(vc), S.after(gp)]
        if P2_STAGE < 4:
            continue
        for kb in range(64):
            b1b = 4 + (kb % 2) * 2
            b2b = 5 + (kb % 2) * 2
            dv = A.f32(ca_off, 8192)
            wt = (list(ddone) if kb == 0 else []) + \
                ([S.after(ev)] if kb >= 2 else [])
            for j in range(2):
                dre = dv[64 * j:64 * j + 64, kb * 128:kb * 128 + 64]
                dim = dv[64 * j:64 * j + 64, kb * 128 + 64:kb * 128 + 128]
                S.op("tensor", lambda e, b=b1b, j=j, m=dre:
                     e.matmul(psum[b][64 * j:64 * j + 64, 0:64],
                              i1ap(0, 64 * j), m, start=True, stop=False,
                              tile_position=(64 * j, 64 * j), **MMKW),
                     waits=wt if j == 0 else [])
                S.op("tensor", lambda e, b=b1b, j=j, m=dim:
                     e.matmul(psum[b][64 * j:64 * j + 64, 0:64],
                              i1ap(2, 64 * j), m, start=False, stop=True,
                              tile_position=(64 * j, 64 * j), **MMKW))
                S.op("tensor", lambda e, b=b2b, j=j, m=dre:
                     e.matmul(psum[b][64 * j:64 * j + 64, 0:64],
                              i1ap(1, 64 * j), m, start=True, stop=False,
                              tile_position=(64 * j, 64 * j), **MMKW))
                S.op("tensor", lambda e, b=b2b, j=j, m=dim:
                     e.matmul(psum[b][64 * j:64 * j + 64, 0:64],
                              i1ap(0, 64 * j), m, start=False, stop=True,
                              tile_position=(64 * j, 64 * j), **MMKW),
                     inc=(mm if j == 1 else None))
            wmm = S.after(mm)
            ev4 = A.f32(b1_off, 8192).rearrange("p (pl dq kb) -> p pl dq kb",
                                                pl=2, kb=64)
            S.op("vector", lambda e, b=b1b, kb=kb, ev4=ev4:
                 e.tensor_copy(ev4[:, 0, :, kb], psum[b][:, 0:64]),
                 waits=[wmm], inc=ev)
            S.op("vector", lambda e, b=b2b, kb=kb, ev4=ev4:
                 e.tensor_copy(ev4[:, 1, :, kb], psum[b][:, 0:64]), inc=ev)
        edone = S.after(ev)
        if P2_STAGE < 5:
            continue
        ev_f = A.f32(b1_off, 8192)
        e2r = A.f32(ca_off, 8192).rearrange("p (pl na j q) -> p pl na j q",
                                            pl=2, na=64, j=2, q=32)
        for pl in range(2):
            for j in range(2):
                for dqh in range(32):
                    bank = (dqh // 8) % 2
                    wt = []
                    if dqh % 8 == 0:
                        wt.append(S.after(ev))
                        if pl == 0 and j == 0 and dqh == 0:
                            wt.append(edone)
                    src_ap = ev_f[64 * j:64 * j + 64,
                                  pl * 4096 + dqh * 128: pl * 4096 + dqh * 128 + 128]
                    S.op("tensor", lambda e, b=bank, dqh=dqh, s=src_ap, j=j:
                         e.transpose(psum[b][:, (dqh % 8) * 64:(dqh % 8) * 64 + 64],
                                     s, ident64[j]),
                         waits=wt, inc=(mm if dqh % 8 == 7 else None))
                    if dqh % 8 == 7:
                        wmm = S.after(mm)
                        dqh0 = dqh - 7
                        dstv = e2r[:, pl, :, j, dqh0:dqh0 + 8] \
                            .rearrange("p na q -> p q na")
                        S.op("vector", lambda e, b=bank, d_=dstv:
                             e.tensor_copy(
                                 d_, psum[b][:, :].rearrange(
                                     "p (blk c) -> p blk c", blk=8)),
                             waits=[wmm], inc=ev)
        e2done = S.after(ev)
        if P2_STAGE < 6:
            continue
        e2f = A.f32(ca_off, 8192)
        for a in range(16):
            for jj in range(4):
                nA = 4 * a + jj
                for dqpar in range(2):
                    bank = (a % 2) * 2 + dqpar
                    mre = e2f[64 * dqpar:64 * dqpar + 64, nA * 64:nA * 64 + 64]
                    mim = e2f[64 * dqpar:64 * dqpar + 64,
                              4096 + nA * 64:4096 + nA * 64 + 64]
                    first = (jj == 0)
                    wt = []
                    if first and dqpar == 0:
                        wt.append(S.after(ev))
                        if a == 0:
                            wt.append(e2done)
                    S.op("tensor", lambda e, b=bank, jj=jj, dqp=dqpar, nA=nA, m=mre:
                         e.matmul(psum[b][32 * jj:32 * jj + 32, 0:64],
                                  i2ap(0, nA, 64 * dqp), m, start=True, stop=False,
                                  tile_position=(64 * dqp, 32 * jj), **MMKW),
                         waits=wt)
                    S.op("tensor", lambda e, b=bank, jj=jj, dqp=dqpar, nA=nA, m=mim:
                         e.matmul(psum[b][32 * jj:32 * jj + 32, 0:64],
                                  i2ap(1, nA, 64 * dqp), m, start=False, stop=True,
                                  tile_position=(64 * dqp, 32 * jj), **MMKW),
                         inc=(mm if (jj == 3 and dqpar == 1) else None))
            wmm = S.after(mm)
            hv = A.f32(hsl_off, 2048).rearrange("p (a d) -> p a d", d=DG)
            for dqpar in range(2):
                bank = (a % 2) * 2 + dqpar
                psv = psum[bank][:, 0:64].rearrange("p (jb qh) -> p jb qh", jb=2)
                dstv = hv[:, a, :].rearrange("p (qh x) -> p qh x", x=4) \
                    [:, :, 2 * dqpar:2 * dqpar + 2].rearrange("p qh jb -> p jb qh")
                S.op("vector", lambda e, d_=dstv, s_=psv: e.tensor_copy(d_, s_),
                     waits=[wmm] if dqpar == 0 else [], inc=ev)
        hslab = S.after(ev)
        hv = A.f32(hsl_off, 2048)
        S.dma("gpsimd",
              h_s[:, grp * DG:(grp + 1) * DG].rearrange("(a p) d -> p a d", p=128),
              hv.rearrange("p (a d) -> p a d", d=DG), waits=[hslab], inc=st)

    A.pop()
    S.fence_all()
    if debug_phase <= 2:
        finish()
        return

    A.push()
    stage_off = A.alloc(1024, "wstage3")
    wy_hi = A.alloc(4 * D, "wy_hi"); wy_lo = A.alloc(4 * D, "wy_lo")
    ht_off = A.alloc(4 * D, "ht")
    mus_off = A.alloc(4 * D, "mus")
    ytn_off = A.alloc(4 * D, "ytn")
    nfh_off = A.alloc(2 * D, "nf_hi")
    nfl_off = A.alloc(2 * D, "nf_lo")
    x1fh_off = A.alloc(2 * D, "x1f_hi")
    x1fl_off = A.alloc(2 * D, "x1f_lo")
    gtd_off = A.alloc(1024, "gtd")
    stt_off = A.alloc(64, "stats")

    wy_done = load_weight_split(T["wyt"], wy_hi, wy_lo, stage_off, stage_n=1)

    for tau in range(4):
        for ca in range(4):
            S.dma("sync", A.f32(ht_off + ca * D, D),
                  h_s[tau * 512 + ca * 128: tau * 512 + ca * 128 + 128, :],
                  waits=[S.after(vc), S.after(st)] if ca == 0 else (), inc=ld)
        hw = S.after(ld)
        for mudram, lcoff, alop in [(T["mur"], SM_LCR, ALU.add),
                                    (T["mui"], SM_LCI, ALU.subtract)]:
            for ca in range(4):
                S.dma("scalar", A.f32(mus_off + ca * D, D),
                      mudram[tau * 512 + ca * 128: tau * 512 + ca * 128 + 128, :],
                      waits=[S.after(vc)] if ca == 0 else (), inc=ld2)
            muw = S.after(ld2)
            for ca in range(4):
                hc_ap = A.f32(ht_off + ca * D, D)
                mu_ap = A.f32(mus_off + ca * D, D)
                wt = [muw, hw] if ca == 0 else []
                S.op("vector", lambda e, m=mu_ap, lo=lcoff: e.tensor_tensor(
                    m, m, small[:, lo:lo + D], ALU.mult), waits=wt)
                S.op("vector", lambda e, h=hc_ap, m=mu_ap, op=alop:
                     e.tensor_tensor(h, h, m, op), inc=vc)
        hready = S.after(vc)
        scr = A.f32(mus_off, D)
        for ca in range(4):
            S.op("scalar", lambda e, ca=ca, o=scr:
                 e.activation(out=o, in_=A.f32(ht_off + ca * D, D), func=AF.Square,
                              accum_out=A.f32(stt_off + ca, 1)),
                 waits=[hready] if ca == 0 else [], inc=(sc if ca == 3 else None))
        S.op("scalar", lambda e: e.activation(
            out=A.f32(stt_off + 8, 4), in_=A.f32(stt_off, 4), func=AF.Sqrt,
            bias=small[:, SM_EPS:SM_EPS + 1], scale=1.0 / D),
            waits=[S.after(sc)], inc=sc)
        S.op("vector", lambda e: e.reciprocal(
            A.f32(stt_off + 12, 4), A.f32(stt_off + 8, 4)),
            waits=[S.after(sc)], inc=vc)
        rsw = S.after(vc)
        for ca in range(4):
            S.op("vector", lambda e, ca=ca: e.scalar_tensor_tensor(
                out=A.f32(ytn_off + ca * D, D), in0=A.f32(ht_off + ca * D, D),
                scalar=A.f32(stt_off + 12 + ca, 1),
                in1=small[:, SM_NWB:SM_NWB + D], op0=ALU.mult, op1=ALU.mult),
                waits=[rsw] if ca == 0 else [], inc=vc)
        ndone = S.after(vc)
        ntdone = transpose_split_tile(ytn_off, nfh_off, nfl_off, first_waits=[ndone])
        for hc in range(NCH):
            par = hc % 2
            S.dma("scalar", A.f32(gtd_off + par * 512, 512),
                  g_s[hc * 128:(hc + 1) * 128, tau * 512:(tau + 1) * 512],
                  waits=[S.after(vc)] if hc >= 2 else
                  ([ntdone] if hc == 0 else ()), inc=ld2)
            gw = S.after(ld2)
            b1 = 2 + (hc % 3) * 2
            extra = [ntdone, wy_done] if hc == 0 else []
            if hc >= 3:
                extra.append(S.after(vc))
            m1 = proj_tile(wy_hi, wy_lo, nfh_off, nfl_off, b1, hc, extra)
            y1b = A.f32(mus_off + par * 512, 512)
            S.op("scalar", lambda e, b=b1, hc=hc, o=y1b:
                 e.activation(out=o, in_=psum[b][:, :], func=AF.Identity,
                              bias=small[:, SM_BY + hc:SM_BY + hc + 1], scale=1.0),
                 waits=[m1], inc=sc)
            S.op("vector", lambda e, hc=hc, i0=y1b, par=par:
                 e.tensor_tensor(A.f32(ytn_off + hc * 512, 512), i0,
                                 A.f32(gtd_off + par * 512, 512), ALU.mult),
                 waits=[S.after(sc), gw], inc=vc)
        ydone = S.after(vc)
        for ca in range(4):
            S.dma("sync", A.f32(mus_off + ca * D, D),
                  xo[tau * 512 + ca * 128: tau * 512 + ca * 128 + 128, :],
                  waits=[ydone] if ca == 0 else (), inc=ld)
        xw = S.after(ld)
        for ca in range(4):
            for dc in range(NCH):
                bank = 0 if dc < 4 else 1
                wt = []
                if dc == 0:
                    wt = [ydone, S.after(vc)] if ca == 0 else [S.after(vc)]
                S.op("tensor", lambda e, b=bank, ca=ca, dc=dc:
                     e.transpose(psum[b][:, (dc % 4) * 128:(dc % 4) * 128 + 128],
                                 A.f32(ytn_off + dc * 512 + ca * 128, 128), ident),
                     waits=wt, inc=(mm if dc == NCH - 1 else None))
            wmm = S.after(mm)
            for half in range(2):
                bank = half
                S.op("vector", lambda e, b=bank, ca=ca, half=half:
                     e.scalar_tensor_tensor(
                         out=A.f32(ht_off + ca * D + half * 512, 512),
                         in0=psum[b][:, :], scalar=1.0,
                         in1=A.f32(mus_off + ca * D + half * 512, 512),
                         op0=ALU.mult, op1=ALU.add),
                     waits=[wmm, xw] if (ca == 0 and half == 0) else [wmm],
                     inc=vc)
        rdone = S.after(vc)
        scr2 = A.f32(ytn_off, D)
        scr3 = A.f32(ytn_off + D, D)
        for ca in range(4):
            S.op("scalar", lambda e, ca=ca, o=scr2:
                 e.activation(out=o, in_=A.f32(ht_off + ca * D, D), func=AF.Square,
                              accum_out=A.f32(stt_off + 24 + ca, 1)),
                 waits=[rdone] if ca == 0 else [])
            S.op("scalar", lambda e, ca=ca, o=scr3:
                 e.activation(out=o, in_=A.f32(ht_off + ca * D, D), func=AF.Identity,
                              accum_out=A.f32(stt_off + 28 + ca, 1)),
                 inc=(sc if ca == 3 else None))
        rsqw = S.after(sc)
        S.op("vector", lambda e: e.tensor_scalar(
            out=A.f32(stt_off + 32, 4), in0=A.f32(stt_off + 28, 4),
            scalar1=1.0 / D, scalar2=None, op0=ALU.mult), waits=[rsqw], inc=vc)
        S.op("vector", lambda e: e.tensor_scalar(
            out=A.f32(stt_off + 36, 4), in0=A.f32(stt_off + 24, 4),
            scalar1=1.0 / D, scalar2=None, op0=ALU.mult),
            waits=[S.after(vc)], inc=vc)
        S.op("scalar", lambda e: e.activation(
            out=A.f32(stt_off + 40, 4), in_=A.f32(stt_off + 32, 4),
            func=AF.Square), waits=[S.after(vc)], inc=sc)
        S.op("vector", lambda e: e.tensor_tensor(
            A.f32(stt_off + 44, 4), A.f32(stt_off + 36, 4),
            A.f32(stt_off + 40, 4), ALU.subtract), waits=[S.after(sc)], inc=vc)
        S.op("scalar", lambda e: e.activation(
            out=A.f32(stt_off + 48, 4), in_=A.f32(stt_off + 44, 4),
            func=AF.Sqrt, bias=small[:, SM_EPS + 1:SM_EPS + 2], scale=1.0),
            waits=[S.after(vc)], inc=sc)
        S.op("vector", lambda e: e.reciprocal(
            A.f32(stt_off + 52, 4), A.f32(stt_off + 48, 4)),
            waits=[S.after(sc)], inc=vc)
        lnw = S.after(vc)
        for ca in range(4):
            S.op("vector", lambda e, ca=ca: e.tensor_scalar(
                out=A.f32(ytn_off + ca * D, D),
                in0=A.f32(ht_off + ca * D, D),
                scalar1=A.f32(stt_off + 32 + ca, 1),
                scalar2=A.f32(stt_off + 52 + ca, 1),
                op0=ALU.subtract, op1=ALU.mult),
                waits=[lnw] if ca == 0 else [], inc=vc)
        x1done = S.after(vc)
        for ca in range(4):
            S.dma("gpsimd", x1_s[tau * 512 + ca * 128: tau * 512 + ca * 128 + 128, :],
                  A.f32(ytn_off + ca * D, D),
                  waits=[x1done] if ca == 0 else (), inc=st)
        x1td = transpose_split_tile(ytn_off, x1fh_off, x1fl_off,
                                    first_waits=[x1done])
        S.dma("gpsimd", x1f_s[0, tau, :, :], A.bf16(x1fh_off, 8 * 512),
              waits=[x1td], inc=st)
        S.dma("gpsimd", x1f_s[1, tau, :, :], A.bf16(x1fl_off, 8 * 512),
              inc=st)

    A.pop()
    S.fence_all()
    if debug_phase <= 3:
        finish()
        return

    A.push()
    w4s_off = A.alloc(1024, "w4stage")
    wf_hi = A.alloc(4 * D, "wf_hi"); wf_lo = A.alloc(4 * D, "wf_lo")
    x1r_off = A.alloc(4 * D, "x1res")
    scA_off = A.alloc(4 * D, "scrA")
    scB_off = A.alloc(4 * D, "scrB")
    ouf_off = A.alloc(2 * D, "outF")
    b9_off = A.alloc(1024, "b9")
    gts_off = A.alloc(2048, "gatesT")
    wr_off = A.alloc(256, "wr")
    sq_off = A.alloc(512, "sqscr")
    st4_off = A.alloc(64, "stats4")
    gsc_off = A.alloc(192, "gates_scr")
    x1qh_off = A.alloc(2 * D, "x1q_hi")
    x1ql_off = A.alloc(2 * D, "x1q_lo")

    for w in range(9):
        wsrc = T["wst"] if w == 0 else T["wet"][(w - 1) * D:w * D, :]
        for c in range(NCH):
            stg = A.f32(w4s_off, D)
            S.dma("sync", stg, wsrc[c * 128:(c + 1) * 128, :],
                  waits=[S.after(st)], inc=ld)
            wld = S.after(ld)
            hi = A.bf16(wf_hi, D)
            lo = A.bf16(wf_lo, D)
            S.op("gpsimd", lambda e, o=hi, i=stg: e.tensor_copy(o, i),
                 waits=[wld, S.after(st)], inc=gp)
            S.op("gpsimd", lambda e, o=lo, i0=stg, i1=hi:
                 e.tensor_tensor(o, i0, i1, ALU.subtract), inc=gp)
            spl = S.after(gp)
            S.dma("gpsimd", wsp[0, w * D + c * 128:w * D + (c + 1) * 128, :], hi,
                  waits=[spl], inc=st)
            S.dma("gpsimd", wsp[1, w * D + c * 128:w * D + (c + 1) * 128, :], lo,
                  inc=st)

    wsp_done = S.after(st)
    S.dma("sync", A.f32(w4s_off, NCH * NE).rearrange("p (c e) -> p c e", e=NE),
          T["wrt"].rearrange("(c p) e -> p c e", p=128),
          waits=[S.after(st)], inc=ld)
    wrw = S.after(ld)
    S.op("gpsimd", lambda e: e.tensor_copy(A.bf16(wr_off, NCH * NE),
                                           A.f32(w4s_off, NCH * NE)),
         waits=[wrw], inc=gp)
    S.op("gpsimd", lambda e: e.tensor_tensor(A.bf16(wr_off + 32, NCH * NE),
                                             A.f32(w4s_off, NCH * NE),
                                             A.bf16(wr_off, NCH * NE),
                                             ALU.subtract), inc=gp)
    wrsplit = S.after(gp)
    S.dma("sync", A.f32(b9_off, D, parts=16), T["b9"][:, :], inc=ld)
    b9w = S.after(ld)
    b9ap = A.f32(b9_off, D, parts=16)

    NW = 9
    for Q in range(4):
        for k, off in [(0, x1qh_off), (1, x1ql_off)]:
            S.dma("sync", A.bf16(off, 8 * 512), x1f_s[k, Q, :, :],
                  waits=[S.after(mm), S.after(vc)] if k == 0 else (), inc=ld)
        x1qw = S.after(ld)
        x1qh = A.bf16(x1qh_off, 8 * 512)
        x1ql = A.bf16(x1ql_off, 8 * 512)
        for i in range(4):
            S.dma("sync", A.f32(x1r_off + i * D, D),
                  x1_s[(4 * Q + i) * 128:(4 * Q + i + 1) * 128, :], inc=ld)
        x1rw = S.after(ld)

        for i in range(4):
            ch = 4 * Q + i
            bank = 2 * i
            wrh = A.bf16(wr_off, NCH * NE)
            wrl = A.bf16(wr_off + 32, NCH * NE)
            for dc in range(NCH):
                base = dc * 512 + i * 128
                sh = x1qh[:, base:base + 128]
                slo = x1ql[:, base:base + 128]
                mh = wrh[:, dc * NE:(dc + 1) * NE]
                mlo = wrl[:, dc * NE:(dc + 1) * NE]
                wt = [x1qw, wrsplit] if (i == 0 and dc == 0) else \
                    ([S.after(vc)] if dc == 0 else [])
                S.op("tensor", lambda e, b=bank, s=sh, m=mh, a=(dc == 0):
                     e.matmul(psum[b][:, 0:NE], s, m, start=a, stop=False, **MMKW),
                     waits=wt)
                S.op("tensor", lambda e, b=bank, s=sh, m=mlo:
                     e.matmul(psum[b][:, 0:NE], s, m, start=False, stop=False,
                              **MMKW))
                S.op("tensor", lambda e, b=bank, s=slo, m=mh, o=(dc == NCH - 1):
                     e.matmul(psum[b][:, 0:NE], s, m, start=False, stop=o, **MMKW),
                     inc=(mm if dc == NCH - 1 else None))
            wmm = S.after(mm)
            gb = gsc_off + i * 48
            scv = A.f32(gb, 8)
            eq1 = A.f32(gb + 8, 8)
            sc2v = A.f32(gb + 16, 8)
            eq2 = A.f32(gb + 24, 8)
            m1c = A.f32(gb + 32, 1)
            m2c = A.f32(gb + 33, 1)
            dmc = A.f32(gb + 34, 1)
            e2c = A.f32(gb + 35, 1)
            denc = A.f32(gb + 36, 1)
            rdc = A.f32(gb + 37, 1)
            w2c = A.f32(gb + 38, 1)

            def vstep(fn, waits=()):
                S.op("vector", fn, waits=[S.after(vc)] + list(waits), inc=vc)

            vstep(lambda e, b=bank, o=scv:
                  e.tensor_tensor(o, psum[b][:, 0:NE],
                                  small[:, SM_BRB:SM_BRB + NE], ALU.add),
                  waits=[wmm])
            vstep(lambda e, o=m1c, a=scv: e.tensor_reduce(out=o, in_=a, axis=AX.X,
                                                          op=ALU.max))
            vstep(lambda e, o=eq1, a=scv, s=m1c: e.tensor_scalar(
                out=o, in0=a, scalar1=s, scalar2=None, op0=ALU.is_equal))
            vstep(lambda e, o=sc2v, a=eq1, b=scv: e.scalar_tensor_tensor(
                out=o, in0=a, scalar=-1e30, in1=b, op0=ALU.mult, op1=ALU.add))
            vstep(lambda e, o=m2c, a=sc2v: e.tensor_reduce(out=o, in_=a, axis=AX.X,
                                                           op=ALU.max))
            vstep(lambda e, o=eq2, a=sc2v, s=m2c: e.tensor_scalar(
                out=o, in0=a, scalar1=s, scalar2=None, op0=ALU.is_equal))
            vstep(lambda e, o=dmc, a=m2c, s=m1c: e.tensor_scalar(
                out=o, in0=a, scalar1=s, scalar2=None, op0=ALU.subtract))
            S.op("scalar", lambda e, o=e2c, a=dmc:
                 e.activation(out=o, in_=a, func=AF.Exp),
                 waits=[S.after(vc)], inc=sc)
            S.op("scalar", lambda e, o=denc, a=e2c: e.add(o, a, 1.0),
                 waits=[S.after(sc)], inc=sc)
            vstep(lambda e, o=rdc, a=denc: e.reciprocal(o, a),
                  waits=[S.after(sc)])
            vstep(lambda e, o=w2c, a=e2c, b=rdc: e.tensor_tensor(o, a, b,
                                                                 ALU.mult))
            gch = small[:, SM_G + i * 16:SM_G + i * 16 + 16]
            g1c = A.f32(gb + 40, 8)
            vstep(lambda e, g=gch: e.memset(g, 0.0))
            vstep(lambda e, g=gch: e.memset(g[:, 8:9], 1.0))
            vstep(lambda e, o=g1c, a=eq1, s=rdc: e.tensor_scalar(
                out=o, in0=a, scalar1=s, scalar2=None, op0=ALU.mult))
            vstep(lambda e, g=gch, a=eq2, s=w2c, b=g1c: e.scalar_tensor_tensor(
                out=g[:, 0:8], in0=a, scalar=s, in1=b,
                op0=ALU.mult, op1=ALU.add))
            gdone = S.after(vc)
            bank2 = 2 * i + 1
            S.op("tensor", lambda e, b=bank2, g=gch:
                 e.transpose(psum[b][0:16, 0:128], g, ident),
                 waits=[gdone, S.after(ev)], inc=mm)
            wmt = S.after(mm)
            S.op("vector", lambda e, b=bank2, i=i:
                 e.tensor_copy(A.f32(gts_off + i * 128, 128, parts=16),
                               psum[b][0:16, 0:128]),
                 waits=[wmt], inc=ev)
        gts_done = S.after(ev)

        for i in range(4):
            for hh in range(2):
                bank = 2 * i + hh
                S.op("tensor", lambda e, b=bank, i=i, hh=hh:
                     e.matmul(psum[b][:, :],
                              A.f32(gts_off + i * 128, 128, parts=16),
                              b9ap[:, hh * 512:(hh + 1) * 512],
                              start=True, stop=True, **MMKW),
                     waits=([gts_done, b9w] if (i == 0 and hh == 0) else []),
                     inc=(mm if (i == 3 and hh == 1) else None))
        wmmB = S.after(mm)
        for i in range(4):
            for hh in range(2):
                bank = 2 * i + hh
                S.op("vector", lambda e, b=bank, i=i, hh=hh:
                     e.scalar_tensor_tensor(
                         out=A.f32(scA_off + i * D + hh * 512, 512),
                         in0=psum[b][:, :], scalar=1.0,
                         in1=A.f32(x1r_off + i * D + hh * 512, 512),
                         op0=ALU.mult, op1=ALU.add),
                     waits=[wmmB, x1rw] if (i == 0 and hh == 0) else [wmmB],
                     inc=vc)
        initdone = S.after(vc)
        prev_off, cur_off = scA_off, scB_off
        for w in range(NW):
            for k, pf in [(0, wf_hi), (1, wf_lo)]:
                S.dma("scalar", A.bf16(pf, 8 * D).rearrange(
                          "p (c h) -> p c h", h=D),
                      wsp[k, w * D:(w + 1) * D, :].rearrange(
                          "(c p) h -> p c h", p=128),
                      waits=[S.after(mm), wsp_done] if k == 0 else (), inc=ld2)
            wfw = S.after(ld2)
            last_w = (w == NW - 1)
            for i in range(4):
                ch = 4 * Q + i
                for hh in range(2):
                    bank = 2 * i + hh
                    for dc in range(NCH):
                        base = dc * 512 + i * 128
                        sh = x1qh[:, base:base + 128]
                        slo = x1ql[:, base:base + 128]
                        mh = A.bf16(wf_hi, 8 * D)[:, dc * D + hh * 512:
                                                      dc * D + hh * 512 + 512]
                        mlo = A.bf16(wf_lo, 8 * D)[:, dc * D + hh * 512:
                                                       dc * D + hh * 512 + 512]
                        wt = []
                        if dc == 0:
                            wt.append(S.after(vc))
                            if i == 0 and hh == 0:
                                wt.append(wfw)
                                if w == 0:
                                    wt.append(initdone)
                        S.op("tensor", lambda e, b=bank, s=sh, m=mh, a=(dc == 0):
                             e.matmul(psum[b][:, :], s, m, start=a, stop=False,
                                      **MMKW), waits=wt)
                        S.op("tensor", lambda e, b=bank, s=sh, m=mlo:
                             e.matmul(psum[b][:, :], s, m, start=False, stop=False,
                                      **MMKW))
                        S.op("tensor", lambda e, b=bank, s=slo, m=mh,
                             o=(dc == NCH - 1):
                             e.matmul(psum[b][:, :], s, m, start=False, stop=o,
                                      **MMKW),
                             inc=(mm if (dc == NCH - 1) else None))
                    wmmw = S.after(mm)
                    gate_scalar = None if w == 0 else \
                        small[:, SM_G + i * 16 + (w - 1): SM_G + i * 16 + w]
                    outap = A.f32(cur_off + i * D + hh * 512, 512)
                    accum = None
                    inap = A.f32(prev_off + i * D + hh * 512, 512)

                    def link(e, b=bank, o=outap, i0=inap, g=gate_scalar, ac=accum):
                        if g is None:
                            return e.scalar_tensor_tensor(
                                out=o, in0=psum[b][:, :], scalar=1.0, in1=i0,
                                op0=ALU.mult, op1=ALU.add, accum_out=ac)
                        return e.scalar_tensor_tensor(
                            out=o, in0=psum[b][:, :], scalar=g, in1=i0,
                            op0=ALU.mult, op1=ALU.add, accum_out=ac)
                    S.op("vector", link, waits=[wmmw], inc=vc)
            prev_off, cur_off = cur_off, prev_off
        accdone = S.after(vc)
        fin_off = prev_off
        for i in range(4):
            for hh in range(2):
                S.op("scalar", lambda e, i=i, hh=hh:
                     e.activation(out=A.f32(sq_off, 512),
                                  in_=A.f32(fin_off + i * D + hh * 512, 512),
                                  func=AF.Square,
                                  accum_out=A.f32(st4_off + 8 + i * 2 + hh, 1)),
                     waits=[accdone] if (i == 0 and hh == 0) else [])
                S.op("scalar", lambda e, i=i, hh=hh:
                     e.activation(out=A.f32(sq_off, 512),
                                  in_=A.f32(fin_off + i * D + hh * 512, 512),
                                  func=AF.Identity,
                                  accum_out=A.f32(st4_off + i * 2 + hh, 1)),
                     inc=(sc if (i == 3 and hh == 1) else None))
        sqw = S.after(sc)
        sv1 = A.f32(st4_off, 8).rearrange("p (c h) -> p c h", h=2)
        sv2 = A.f32(st4_off + 8, 8).rearrange("p (c h) -> p c h", h=2)
        S.op("vector", lambda e: e.tensor_tensor(
            A.f32(st4_off + 16, 4), sv1[:, :, 0], sv1[:, :, 1], ALU.add),
            waits=[sqw], inc=vc)
        S.op("vector", lambda e: e.tensor_tensor(
            A.f32(st4_off + 20, 4), sv2[:, :, 0], sv2[:, :, 1], ALU.add),
            waits=[S.after(vc)], inc=vc)
        S.op("vector", lambda e: e.tensor_scalar(
            out=A.f32(st4_off + 24, 4), in0=A.f32(st4_off + 16, 4),
            scalar1=1.0 / D, scalar2=None, op0=ALU.mult),
            waits=[S.after(vc)], inc=vc)
        S.op("vector", lambda e: e.tensor_scalar(
            out=A.f32(st4_off + 28, 4), in0=A.f32(st4_off + 20, 4),
            scalar1=1.0 / D, scalar2=None, op0=ALU.mult),
            waits=[S.after(vc)], inc=vc)
        S.op("scalar", lambda e: e.activation(
            out=A.f32(st4_off + 32, 4), in_=A.f32(st4_off + 24, 4),
            func=AF.Square), waits=[S.after(vc)], inc=sc)
        S.op("vector", lambda e: e.tensor_tensor(
            A.f32(st4_off + 36, 4), A.f32(st4_off + 28, 4),
            A.f32(st4_off + 32, 4), ALU.subtract), waits=[S.after(sc)], inc=vc)
        S.op("scalar", lambda e: e.activation(
            out=A.f32(st4_off + 40, 4), in_=A.f32(st4_off + 36, 4),
            func=AF.Sqrt, bias=small[:, SM_EPS + 1:SM_EPS + 2], scale=1.0),
            waits=[S.after(vc)], inc=sc)
        S.op("vector", lambda e: e.reciprocal(
            A.f32(st4_off + 44, 4), A.f32(st4_off + 40, 4)),
            waits=[S.after(sc)], inc=vc)
        ln2w = S.after(vc)
        for i in range(4):
            ch = 4 * Q + i
            par = i % 2
            S.op("vector", lambda e, i=i, par=par: e.tensor_scalar(
                out=A.f32(ouf_off + par * D, D),
                in0=A.f32(fin_off + i * D, D),
                scalar1=A.f32(st4_off + 24 + i, 1),
                scalar2=A.f32(st4_off + 44 + i, 1),
                op0=ALU.subtract, op1=ALU.mult),
                waits=([ln2w] if i == 0 else []) +
                ([S.after(st)] if i >= 2 else []), inc=vc)
            S.dma("gpsimd", out_t[ch * 128:(ch + 1) * 128, :],
                  A.f32(ouf_off + par * D, D), waits=[S.after(vc)], inc=st)

    A.pop()
    finish()



def _host_tables():
    scale = _scale_bits()
    K, MU32 = _c64_tables(scale)
    Kf_lo = _kf_spec(K[:HALF])
    Kf_hi = _kf_spec(K[HALF:])
    sgn = (-1.0) ** (np.arange(64))[None, :, None]
    TA1 = _table_layout(Kf_lo)
    TA2 = _table_layout(Kf_hi + Kf_lo * sgn)
    w1, fstats = _fft_const_tables()
    return {
        "MU32": MU32, "TA1": TA1, "TA2": TA2,
        "TB1": np.zeros_like(TA1), "TB2": TA1,
        "w1": w1, "fstats": fstats,
    }


def _make_in_maps(inputs, tabs):
    f32 = np.float32
    x = np.ascontiguousarray(np.asarray(inputs["L0_attn_x"], dtype=f32))

    def wT(w):
        return np.ascontiguousarray(np.asarray(w, dtype=f32).T)

    def col8(v):
        return np.ascontiguousarray(np.asarray(v, dtype=f32).reshape(NCH, 128).T)

    wet = np.ascontiguousarray(
        np.stack([np.asarray(inputs["L0_ffn_W_experts"][e], dtype=f32).T
                  for e in range(NE)]).reshape(NE * D, D))
    b9 = np.zeros((16, D), dtype=f32)
    b9[:NE] = np.asarray(inputs["L0_ffn_b_experts"], dtype=f32)
    b9[NE] = np.asarray(inputs["L0_ffn_b_shared"], dtype=f32)

    def bcast(v):
        v = np.asarray(v, dtype=f32)
        return np.ascontiguousarray(np.broadcast_to(v[None, :], (128, v.shape[0])))

    MU32 = tabs["MU32"]
    common = dict(
        wzt=wT(inputs["L0_attn_Wz"]), wzat=wT(inputs["L0_attn_Wza"]),
        wyt=wT(inputs["L0_attn_Wy"]), wyat=wT(inputs["L0_attn_Wya"]),
        wst=wT(inputs["L0_ffn_W_shared"]), wet=wet,
        wrt=np.ascontiguousarray(np.asarray(inputs["L0_ffn_W_router"], dtype=f32).T),
        bz_c=col8(inputs["L0_attn_bz"]), bza_c=col8(inputs["L0_attn_bza"]),
        by_c=col8(inputs["L0_attn_by"]), bya_c=col8(inputs["L0_attn_bya"]),
        b9=b9, brb=bcast(inputs["L0_ffn_b_router"]),
        nwb=bcast(inputs["L0_attn_norm_w"]),
        lcreb=bcast(inputs["L0_attn_lc_re"]), lcimb=bcast(inputs["L0_attn_lc_im"]),
        w1t=tabs["w1"], fstats=tabs["fstats"],
        identt=np.concatenate(
            [np.eye(128, dtype=f32),
             np.concatenate([np.eye(64, dtype=f32)] * 2, axis=0)], axis=1),
        epsc=np.broadcast_to(np.array([1e-6, 1e-5], dtype=f32)[None, :],
                             (128, 2)).copy(),
    )
    in_maps = []
    for core in range(8):
        b, hf = core // 2, core % 2
        mu = MU32[hf * HALF:(hf + 1) * HALF][FUNKY]
        m = dict(common)
        m.update(
            x_full=np.ascontiguousarray(x[b]),
            x_own=np.ascontiguousarray(x[b, hf * HALF:(hf + 1) * HALF][FUNKY]),
            mur=np.ascontiguousarray(mu.real.astype(f32)),
            mui=np.ascontiguousarray(mu.imag.astype(f32)),
            tat=tabs["TA1"] if hf == 0 else tabs["TA2"],
            tbt=tabs["TB1"] if hf == 0 else tabs["TB2"],
        )
        in_maps.append(m)
    return in_maps


def kernel(**inputs):
    tabs = _host_tables()
    nc = build_nc()
    in_maps = _make_in_maps(inputs, tabs)
    res = run_bass_kernel_spmd(nc, in_maps, core_ids=list(range(8)))
    out = np.zeros((B, L, D), dtype=np.float32)
    for core in range(8):
        b, hf = core // 2, core % 2
        o = np.asarray(res.results[core]["out"])
        out[b, hf * HALF:(hf + 1) * HALF] = o[INV_FUNKY]
    return out
